# version 01
# baseline (speedup 1.0000x reference)
import contextlib
import math
import numpy as np
import ml_dtypes
import concourse.bass as bass
import concourse.mybir as mybir
from concourse.alu_op_type import AluOpType
from concourse.bass_utils import run_bass_kernel_spmd

F32 = mybir.dt.float32
BF16 = mybir.dt.bfloat16
AF = mybir.ActivationFunctionType
OP = AluOpType
NBF = ml_dtypes.bfloat16

NCORES = 8
B = 16
S = 2048
D = 1024
DH = 4096
B_LOC = B // NCORES
T = B_LOC * S
DC = D // 128
HC = DH // 128
NT = T // 512
LN_EPS = 1e-5
RMS_EPS = 1e-5
LAMBDA_INIT = 0.8 - 0.6 * math.exp(-0.3 * 12)


class Tok:
    __slots__ = ("eng", "sem", "val", "seq", "dma", "narrow")

    def __init__(self, eng, sem, val, seq, dma=False, narrow=False):
        self.eng, self.sem, self.val, self.seq, self.dma = eng, sem, val, seq, dma
        self.narrow = narrow


class Prog:
    ENGINES = ("sync", "tensor", "vector", "scalar", "gpsimd")

    def __init__(self, nc, stack):
        self.nc = nc
        self.ops = {e: [] for e in self.ENGINES}
        self.seq = {e: 0 for e in self.ENGINES}
        self.sems = {}
        self.sem_cnt = {}
        for e in self.ENGINES:
            s = stack.enter_context(nc.semaphore("sem_" + e))
            self.sems[e] = s
            self.sem_cnt[e] = 0

    def emit(self, eng, fn, deps=(), produce=False, dma=False, narrow=False):
        waits = {}
        my_seq = self.seq[eng]
        for t in deps:
            if t is None:
                continue
            if t.eng == eng:
                assert t.seq < my_seq, "dep on later same-engine op"
                if not t.dma and not t.narrow:
                    continue
            if waits.get(t.sem, 0) < t.val:
                waits[t.sem] = t.val
        tok = None
        inc_n = 16 if dma else 1
        if produce:
            assert not (narrow and not produce)
            self.sem_cnt[eng] += inc_n
            tok = Tok(eng, self.sems[eng], self.sem_cnt[eng], my_seq, dma, narrow)
        w = list(waits.items())
        sem = self.sems[eng]

        def thunk(engine, fn=fn, w=w, produce=produce, inc_n=inc_n, sem=sem):
            for s, v in w:
                engine.wait_ge(s, v)
            instr = fn(engine)
            if produce:
                instr.then_inc(sem, inc_n)
        self.ops[eng].append(thunk)
        self.seq[eng] += 1
        return tok

    def replay(self, block):
        for e in self.ENGINES:
            if not self.ops[e]:
                continue
            section = getattr(block, e)

            def make(ops):
                def _(engine):
                    for f in ops:
                        f(engine)
                return _
            section(make(self.ops[e]))


class Arena:

    def __init__(self, nc, stack, name, nbytes):
        self.h = stack.enter_context(
            nc.sbuf_tensor(name, [128, nbytes], mybir.dt.uint8))
        self.nbytes = nbytes

    def view(self, off, dtype, cols):
        esz = mybir.dt.size(dtype)
        assert off % 32 == 0, off
        assert off + cols * esz <= self.nbytes, (off, cols, esz, self.nbytes)
        return _View(self.h[:, off:off + cols * esz].bitcast(dtype))


class _View:
    __slots__ = ("_ap",)

    def __init__(self, ap):
        self._ap = ap

    def ap(self):
        return self._ap


def host_prep(inputs):
    a = {k: np.asarray(v) for k, v in inputs.items()}
    pr = {}

    x = np.ascontiguousarray(a["L0_attn_x"].astype(np.float32, copy=False))
    pr["x_fm"] = [np.ascontiguousarray(
        x[c * B_LOC:(c + 1) * B_LOC].reshape(T, D).T) for c in range(NCORES)]

    perm = np.concatenate([np.arange(0, D, 2), np.arange(1, D, 2)])
    pr["l0_wqt"] = np.ascontiguousarray(a["L0_attn_w_q"].T[:, perm])
    pr["l0_wkt"] = np.ascontiguousarray(a["L0_attn_w_k"].T[:, perm])
    pr["l0_wvt"] = np.ascontiguousarray(a["L0_attn_w_v"].T)

    i = np.arange(D // 2, dtype=np.float32)
    theta = np.float32(10000.0) ** (np.float32(-2.0) * (i - np.float32(1.0)) / np.float32(D))
    pos = np.arange(S, dtype=np.float32)
    ang32 = (pos[:, None] * theta[None, :]).astype(np.float32)
    pr["rope_cos"] = np.ascontiguousarray(np.cos(np.float64(ang32)).astype(np.float32).T)
    pr["rope_sin"] = np.ascontiguousarray(np.sin(np.float64(ang32)).astype(np.float32).T)

    for L in (0, 1):
        p = f"L{L}_ffn_"
        pr[f"l{L}_w1t"] = np.ascontiguousarray(a[p + "w1_w"].T)
        pr[f"l{L}_w2t"] = np.ascontiguousarray(a[p + "w2_w"].T)
        pr[f"l{L}_h1t"] = np.ascontiguousarray(a[p + "h1_w"].T)
        pr[f"l{L}_h2t"] = np.ascontiguousarray(a[p + "h2_w"].T)
        pr[f"l{L}_w3t"] = np.ascontiguousarray(a[p + "w3_w"].T)
        pr[f"l{L}_s1"] = np.ascontiguousarray(a[p + "w1_s"].reshape(HC, 128).T)
        pr[f"l{L}_s2"] = np.ascontiguousarray(a[p + "w2_s"].reshape(HC, 128).T)
        pr[f"l{L}_sh1"] = np.ascontiguousarray(a[p + "h1_s"].reshape(HC, 128).T)
        pr[f"l{L}_sh2"] = np.ascontiguousarray(a[p + "h2_s"].reshape(HC, 128).T)
        pr[f"l{L}_s3"] = np.ascontiguousarray(a[p + "w3_s"].reshape(DC, 128).T)

    pr["l1_wq"] = np.ascontiguousarray(a["L1_attn_Wq"])
    pr["l1_wk"] = np.ascontiguousarray(a["L1_attn_Wk"])
    pr["l1_wv"] = np.ascontiguousarray(a["L1_attn_Wv"])
    pr["l1_wo"] = np.ascontiguousarray(a["L1_attn_Wo"])
    pr["l1_lq1"] = np.ascontiguousarray(a["L1_attn_lq1"].reshape(1, 32))
    pr["l1_lk1"] = np.ascontiguousarray(a["L1_attn_lk1"].reshape(1, 32))
    pr["l1_lq2"] = np.ascontiguousarray(a["L1_attn_lq2"].reshape(1, 32))
    pr["l1_lk2"] = np.ascontiguousarray(a["L1_attn_lk2"].reshape(1, 32))
    pr["l1_rmsw"] = np.ascontiguousarray(
        np.tile(a["L1_attn_rms_w"].reshape(64), 2).reshape(128, 1))
    return pr


def emit_ln_tile(p, v, tt, z_toks, x_out_r, last, psA, psB, psC, psD,
                 xbf_out=None):
    zst, zsq, zb = v["zst"], v["zsq"], v["zb"]
    prev_s2 = last.get("ln_s2_read")
    sq_prev = None
    tS2 = tS1 = None
    for oc in range(DC):
        tQ = p.emit("scalar", lambda e, oc=oc:
                    e.activation(out=zsq.ap(), in_=zst.ap()[:, oc * 512:(oc + 1) * 512],
                                 func=AF.Square),
                    deps=[z_toks[oc]] + ([sq_prev] if sq_prev else [])
                    + ([last.get("sq_rd")] if oc == 0 else []),
                    produce=True)
        tC = p.emit("vector", lambda e, oc=oc:
                    e.tensor_copy(out=zb.ap()[:, oc * 512:(oc + 1) * 512],
                                  in_=zst.ap()[:, oc * 512:(oc + 1) * 512]),
                    deps=[z_toks[oc]] + ([last.get("ln_zb_free")] if oc == 0 else []),
                    produce=True)
        tS2 = p.emit("tensor", lambda e, oc=oc:
                     e.matmul(psA.ap()[0:1, :], v["ones_bf"].ap(), zsq.ap(),
                              start=(oc == 0), stop=(oc == DC - 1)),
                     deps=[tQ] + ([prev_s2] if oc == 0 else [])
                     + (last.get("consts", []) if oc == 0 else []),
                     produce=True)
        sq_prev = tS2
        tS1 = p.emit("tensor", lambda e, oc=oc:
                     e.matmul(psB.ap()[0:1, :], v["ones_bf"].ap(),
                              zb.ap()[:, oc * 512:(oc + 1) * 512],
                              start=(oc == 0), stop=(oc == DC - 1)),
                     deps=[tC] + ([prev_s2] if oc == 0 else []),
                     produce=True)
    last["ln_zb_free"] = tS1
    last["sq_rd"] = tS2

    tm = p.emit("vector", lambda e:
                e.tensor_scalar(out=v["mrow"].ap()[0:1, :], in0=psB.ap()[0:1, :],
                                scalar1=1.0 / D, scalar2=None, op0=OP.mult),
                deps=[tS1] + last.get("ln_mrow_free", []), produce=True)
    last["ln_s1_read"] = tm
    te2 = p.emit("vector", lambda e:
                 e.tensor_scalar(out=v["r_e2"].ap()[0:1, :], in0=psA.ap()[0:1, :],
                                 scalar1=1.0 / D, scalar2=None, op0=OP.mult),
                 deps=[tS2], produce=True)
    last["ln_s2_read"] = te2
    tmsq = p.emit("scalar", lambda e:
                  e.activation(out=v["r_msq"].ap()[0:1, :], in_=v["mrow"].ap()[0:1, :],
                               func=AF.Square),
                  deps=[tm, last.get("ln_msq_free")], produce=True)
    tvar = p.emit("vector", lambda e:
                  e.scalar_tensor_tensor(out=v["r_var"].ap()[0:1, :],
                                         in0=v["r_e2"].ap()[0:1, :],
                                         scalar=v["eps1"].ap()[0:1, :],
                                         in1=v["r_msq"].ap()[0:1, :],
                                         op0=OP.add, op1=OP.subtract),
                  deps=[te2, tmsq], produce=True)
    last["ln_msq_free"] = tvar
    tln = p.emit("scalar", lambda e:
                 e.activation(out=v["r_rstd"].ap()[0:1, :], in_=v["r_var"].ap()[0:1, :],
                              func=AF.Ln),
                 deps=[tvar, last.get("ln_rstd_free")], produce=True)
    trs = p.emit("scalar", lambda e:
                 e.activation(out=v["r_rstd"].ap()[0:1, :], in_=v["r_rstd"].ap()[0:1, :],
                              func=AF.Exp, scale=-0.5),
                 deps=[tln], produce=True)
    tbm = p.emit("tensor", lambda e:
                 e.matmul(psC.ap(), v["ones_row"].ap()[0:1, :], v["mrow"].ap()[0:1, :],
                          start=True, stop=True),
                 deps=[tm, last.get("ln_bc_free")], produce=True)
    tbr = p.emit("tensor", lambda e:
                 e.matmul(psD.ap(), v["ones_row"].ap()[0:1, :], v["r_rstd"].ap()[0:1, :],
                          start=True, stop=True),
                 deps=[trs, last.get("ln_bc_free")], produce=True)
    last["ln_mrow_free"] = [tbm, tbr]
    last["ln_rstd_free"] = tbr
    tcm = p.emit("vector", lambda e: e.tensor_copy(out=v["bc_m"].ap(), in_=psC.ap()),
                 deps=[tbm], produce=True)
    tcr = p.emit("vector", lambda e: e.tensor_copy(out=v["bc_r"].ap(), in_=psD.ap()),
                 deps=[tbr], produce=True)
    last["ln_bc_free"] = tcr

    st_toks = []
    for oc in range(DC):
        slot = oc % 2
        of = v["out_f"][slot]
        tsub = p.emit("vector", lambda e, oc=oc, of=of:
                      e.tensor_tensor(out=of.ap(),
                                      in0=zst.ap()[:, oc * 512:(oc + 1) * 512],
                                      in1=v["bc_m"].ap(), op=OP.subtract),
                      deps=[tcm] + last.get(("outf", slot), []),
                      produce=True)
        tmul = p.emit("vector", lambda e, of=of:
                      e.tensor_tensor(out=of.ap(), in0=of.ap(),
                                      in1=v["bc_r"].ap(), op=OP.mult),
                      deps=[tcr], produce=True)
        tst = p.emit("sync", lambda e, oc=oc, of=of, tt=tt:
                     e.dma_start(out=x_out_r[oc, :, tt * 512:(tt + 1) * 512],
                                 in_=of.ap()),
                     deps=[tmul], produce=True, dma=True)
        users = [tst]
        if xbf_out is not None:
            tbf = p.emit("vector", lambda e, oc=oc, of=of, tt=tt:
                         e.tensor_copy(
                             out=xbf_out.ap()[:, oc * T + tt * 512: oc * T + tt * 512 + 512],
                             in_=of.ap()),
                         deps=[tmul], produce=True)
            users.append(tbf)
        last[("outf", slot)] = users
        last[("zuse", oc)] = tsub
        st_toks.extend(users)
    return st_toks


def emit_ffn(p, d, L, ar_stream, ar_w, ar_stage, ar_const, psum,
             x32_in, x32_out, x_bf=None, x_bf_ready=None, xbf_out=None,
             last=None):
    nc = p.nc
    pre = f"l{L}_"
    if last is None:
        last = {}

    xbf = ar_stream.view(0, BF16, DC * T)
    wblk = [ar_w.view(i * DC * 128 * 2, BF16, DC * 128) for i in range(4)]
    h0st = [ar_w.view(16 * 1024 + i * 8 * 1024, BF16, T) for i in range(2)]
    wbig = ar_w.view(0, BF16, HC * 2048)
    htile = [ar_stream.view(i * 32 * 1024, BF16, HC * 512) for i in range(2)]
    S4 = 64 * 1024
    x32t = [ar_w.view(S4 + 16 * 1024 + i * 16 * 1024, F32, DC * 512) for i in range(2)]
    v = {
        "zst": ar_w.view(S4, F32, DC * 512),
        "zb": ar_w.view(S4 + 48 * 1024, BF16, DC * 512),
        "bc_m": ar_w.view(S4 + 56 * 1024, F32, 512),
        "bc_r": ar_w.view(S4 + 58 * 1024, F32, 512),
        "out_f": [ar_w.view(S4 + 60 * 1024 + i * 2 * 1024, F32, 512)
                  for i in range(2)],
        "zsq": ar_stage.view(4 * 1024, BF16, 512),
        "mrow": ar_stage.view(5 * 1024, F32, 512),
        "r_e2": ar_stage.view(7 * 1024, F32, 512),
        "r_msq": ar_stage.view(9 * 1024, F32, 512),
        "r_rstd": ar_stage.view(11 * 1024, F32, 512),
        "ones_bf": ar_const.view(0, BF16, 1),
        "ones_row": ar_const.view(32, F32, 128),
        "eps1": ar_const.view(576, F32, 1),
    }
    v["r_var"] = v["r_e2"]
    ost = [ar_stage.view(i * 1024, BF16, 512) for i in range(4)]
    s1c = ar_const.view(1024, F32, HC)
    s2c = ar_const.view(1024 + 128, F32, HC)
    sh1c = ar_const.view(1024 + 256, F32, HC)
    sh2c = ar_const.view(1024 + 384, F32, HC)
    s3c = ar_const.view(1024 + 512, F32, DC)

    if "consts" not in last:
        t_c1 = p.emit("vector", lambda e: e.memset(v["ones_bf"].ap(), 1.0), produce=True)
        t_c2 = p.emit("vector", lambda e: e.memset(v["ones_row"].ap()[0:1, :], 1.0), produce=True)
        t_c3 = p.emit("vector", lambda e: e.memset(v["eps1"].ap()[0:1, :], LN_EPS), produce=True)
        last["consts"] = [t_c1, t_c2, t_c3]
    t_scl = {}
    for dst, key in ((s1c, "s1"), (s2c, "s2"), (sh1c, "sh1"), (sh2c, "sh2"), (s3c, "s3")):
        t_scl[key] = p.emit("gpsimd", lambda e, dst=dst, key=key:
                            e.dma_start(out=dst.ap(), in_=d[pre + key]),
                            produce=True, dma=True)

    if x_bf is None:
        x_in_r = x32_in.rearrange("(c p) t -> c p t", p=128)
        x_bf_ready = []
        for c in range(DC):
            x_bf_ready.append(p.emit("gpsimd", lambda e, c=c:
                                     e.dma_start(out=xbf.ap()[:, c * T:(c + 1) * T],
                                                 in_=x_in_r[c]),
                                     produce=True, dma=True))
        x_bf = xbf
    else:
        assert x_bf_ready is not None

    w1_r = d[pre + "w1t"].rearrange("(k p) m -> p k m", p=128)
    w2_r = d[pre + "w2t"].rearrange("(k p) m -> p k m", p=128)
    h0_d = d[pre + "h0"]

    wtok = {}

    def load_wblk(m, dep):
        b = (m % 2) * 2
        wtok[("w1", m)] = p.emit("gpsimd", lambda e, m=m, b=b:
                                 e.dma_start(out=wblk[b].ap(),
                                             in_=w1_r[:, :, m * 128:(m + 1) * 128]),
                                 deps=dep, produce=True, dma=True)
        wtok[("w2", m)] = p.emit("gpsimd", lambda e, m=m, b=b:
                                 e.dma_start(out=wblk[b + 1].ap(),
                                             in_=w2_r[:, :, m * 128:(m + 1) * 128]),
                                 deps=dep, produce=True, dma=True)

    load_wblk(0, [])
    load_wblk(1, [])
    h0_store_toks = []
    for m in range(HC):
        b = (m % 2) * 2
        hbuf = h0st[m % 2]
        first_deps = [wtok[("w1", m)], wtok[("w2", m)], t_scl["s1"], t_scl["s2"]] + x_bf_ready
        for tt in range(NT):
            pa, pb = psum[tt % 2], psum[2 + tt % 2]
            depsA = (first_deps if tt == 0 else []) + [last.get(("bank", tt % 2))]
            tA = None
            for k in range(DC):
                tA = p.emit("tensor", lambda e, k=k, tt=tt, b=b, pa=pa:
                            e.matmul(pa.ap(), wblk[b].ap()[:, k * 128:(k + 1) * 128],
                                     x_bf.ap()[:, k * T + tt * 512: k * T + tt * 512 + 512],
                                     start=(k == 0), stop=(k == DC - 1)),
                            deps=depsA if k == 0 else [],
                            produce=(k == DC - 1))
            depsB = [last.get(("bank", 2 + tt % 2))]
            tB = None
            for k in range(DC):
                tB = p.emit("tensor", lambda e, k=k, tt=tt, b=b, pb=pb:
                            e.matmul(pb.ap(), wblk[b + 1].ap()[:, k * 128:(k + 1) * 128],
                                     x_bf.ap()[:, k * T + tt * 512: k * T + tt * 512 + 512],
                                     start=(k == 0), stop=(k == DC - 1)),
                            deps=depsB if k == 0 else [],
                            produce=(k == DC - 1))
            slot = tt % 4
            tS = p.emit("scalar", lambda e, tt=tt, m=m, pa=pa, slot=slot:
                        e.activation(out=ost[slot].ap(), in_=pa.ap(), func=AF.Silu,
                                     scale=s1c.ap()[:, m:m + 1]),
                        deps=[tA] + last.get(("ost", slot), []),
                        produce=True)
            last[("bank", tt % 2)] = tS
            tE = p.emit("vector", lambda e, tt=tt, m=m, pb=pb, hbuf=hbuf, slot=slot:
                        e.scalar_tensor_tensor(
                            out=hbuf.ap()[:, tt * 512:(tt + 1) * 512],
                            in0=pb.ap(), scalar=s2c.ap()[:, m:m + 1],
                            in1=ost[slot].ap(), op0=OP.mult, op1=OP.mult),
                        deps=[tB, tS] + ([last.get(("h0buf", m % 2))] if tt == 0 else []),
                        produce=True)
            last[("bank", 2 + tt % 2)] = tE
            last[("ost", slot)] = [tE]
        tst = p.emit("sync", lambda e, m=m, hbuf=hbuf:
                     e.dma_start(out=h0_d[m], in_=hbuf.ap()),
                     deps=[tE], produce=True, dma=True)
        last[("h0buf", m % 2)] = tst
        h0_store_toks.append(tst)
        if m + 2 < HC:
            load_wblk(m + 2, [tE])

    def emit_mid_stage(wt_key, sc_view, sc_tok, hin_d, hout_d, start_deps):
        wt_r = d[pre + wt_key].rearrange("(k p) m -> p k m", p=128)
        out_toks = []
        for mh in range(2):
            wdeps = start_deps if mh == 0 else [last["mid_done"]]
            tws = []
            for kc in range(HC):
                tws.append(p.emit("gpsimd", lambda e, kc=kc, mh=mh:
                                  e.dma_start(out=wbig.ap()[:, kc * 2048:(kc + 1) * 2048],
                                              in_=wt_r[:, kc, mh * 2048:(mh + 1) * 2048]),
                                  deps=wdeps if kc == 0 else [], produce=True, dma=True))
            for tt in range(NT):
                slot = tt % 2
                ld = []
                ldep = ([last[("htile", slot)]] if ("htile", slot) in last else []) \
                    + (start_deps if (mh, tt) == (0, 0) else [])
                for g in range(4):
                    ld.append(p.emit("gpsimd", lambda e, tt=tt, g=g, slot=slot:
                                     e.dma_start(
                                         out=htile[slot].ap()[:, g * 8 * 512:(g + 1) * 8 * 512],
                                         in_=hin_d[g * 8:(g + 1) * 8, :, tt * 512:(tt + 1) * 512]
                                         .rearrange("k p t -> p k t")),
                                     deps=ldep if g == 0 else [],
                                     produce=True, dma=True))
                tG = None
                for mc in range(16):
                    bank = (tt * 16 + mc) % 4
                    pbank = psum[bank]
                    mdeps = (ld + tws + [sc_tok] if mc == 0 else []) + [last.get(("bank", bank))]
                    tM = None
                    for kc in range(HC):
                        tM = p.emit("tensor", lambda e, kc=kc, mc=mc, slot=slot, pbank=pbank:
                                    e.matmul(pbank.ap(),
                                             wbig.ap()[:, kc * 2048 + mc * 128: kc * 2048 + mc * 128 + 128],
                                             htile[slot].ap()[:, kc * 512:(kc + 1) * 512],
                                             start=(kc == 0), stop=(kc == HC - 1)),
                                    deps=mdeps if kc == 0 else [],
                                    produce=(kc == HC - 1))
                    m_glob = mh * 16 + mc
                    oslot = mc % 4
                    tG = p.emit("scalar", lambda e, m_glob=m_glob, oslot=oslot, pbank=pbank:
                                e.activation(out=ost[oslot].ap(), in_=pbank.ap(),
                                             func=AF.Gelu,
                                             scale=sc_view.ap()[:, m_glob:m_glob + 1]),
                                deps=[tM] + last.get(("ost", oslot), []),
                                produce=True)
                    last[("bank", bank)] = tG
                    tSt = p.emit("sync", lambda e, m_glob=m_glob, tt=tt, oslot=oslot:
                                 e.dma_start(out=hout_d[m_glob, :, tt * 512:(tt + 1) * 512],
                                             in_=ost[oslot].ap()),
                                 deps=[tG], produce=True, dma=True)
                    last[("ost", oslot)] = [tSt]
                    out_toks.append(tSt)
                last[("htile", slot)] = tG
            last["mid_done"] = tG
        return out_toks

    s2_toks = emit_mid_stage("h1t", sh1c, t_scl["sh1"], h0_d, d[pre + "h1"],
                             [h0_store_toks[-1]])
    s3_toks = emit_mid_stage("h2t", sh2c, t_scl["sh2"], d[pre + "h1"], d[pre + "h2"],
                             [s2_toks[-1]])

    w3_r = d[pre + "w3t"].rearrange("(k p) m -> p k m", p=128)
    h2_d = d[pre + "h2"]
    x_in_r = x32_in.rearrange("(c p) t -> c p t", p=128)
    x_out_r = x32_out.rearrange("(c p) t -> c p t", p=128)

    tw3 = []
    for kc in range(HC):
        tw3.append(p.emit("gpsimd", lambda e, kc=kc:
                          e.dma_start(out=wbig.ap()[:, kc * 1024:(kc + 1) * 1024],
                                      in_=w3_r[:, kc, :]),
                          deps=[s3_toks[-1], last["mid_done"]] if kc == 0 else [],
                          produce=True, dma=True))

    out_toks = []
    for tt in range(NT):
        slot = tt % 2
        ld = []
        ldep = [last[("htile", slot)]] if ("htile", slot) in last else []
        for g in range(4):
            ld.append(p.emit("gpsimd", lambda e, tt=tt, g=g, slot=slot:
                             e.dma_start(
                                 out=htile[slot].ap()[:, g * 8 * 512:(g + 1) * 8 * 512],
                                 in_=h2_d[g * 8:(g + 1) * 8, :, tt * 512:(tt + 1) * 512]
                                 .rearrange("k p t -> p k t")),
                             deps=ldep if g == 0 else [], produce=True, dma=True))
        tx = p.emit("gpsimd", lambda e, tt=tt, slot=slot:
                    e.dma_start(out=x32t[slot].ap(),
                                in_=x_in_r[:, :, tt * 512:(tt + 1) * 512]
                                .rearrange("c p t -> p c t")),
                    deps=[last.get(("x32t", slot))], produce=True, dma=True)
        z_toks = []
        tZlast = None
        for oc in range(DC):
            bank = oc % 4
            pbank = psum[bank]
            mdeps = (ld + tw3 + [t_scl["s3"]] if oc == 0 else []) + [last.get(("bank", bank))]
            tM = None
            for kc in range(HC):
                tM = p.emit("tensor", lambda e, kc=kc, oc=oc, slot=slot, pbank=pbank:
                            e.matmul(pbank.ap(),
                                     wbig.ap()[:, kc * 1024 + oc * 128: kc * 1024 + oc * 128 + 128],
                                     htile[slot].ap()[:, kc * 512:(kc + 1) * 512],
                                     start=(kc == 0), stop=(kc == HC - 1)),
                            deps=mdeps if kc == 0 else [],
                            produce=(kc == HC - 1))
            tZ = p.emit("vector", lambda e, oc=oc, slot=slot, pbank=pbank:
                        e.scalar_tensor_tensor(
                            out=v["zst"].ap()[:, oc * 512:(oc + 1) * 512],
                            in0=pbank.ap(), scalar=s3c.ap()[:, oc:oc + 1],
                            in1=x32t[slot].ap()[:, oc * 512:(oc + 1) * 512],
                            op0=OP.mult, op1=OP.add),
                        deps=[tM, tx, last.get(("zuse", oc))], produce=True)
            last[("bank", bank)] = tZ
            z_toks.append(tZ)
            tZlast = tZ
        last[("htile", slot)] = tZlast
        last[("x32t", slot)] = tZlast
        st = emit_ln_tile(p, v, tt, z_toks, x_out_r, last,
                          psum[4], psum[5], psum[6], psum[7], xbf_out=xbf_out)
        out_toks.extend(st)
    return out_toks


def emit_attn0(p, d, ar_stream, ar_w, ar_stage, ar_const, psum, x32_in, x32_out,
               last=None, dbg=None, trunc=False):
    if last is None:
        last = {}
    SL = S
    NG = SL // 512
    sc_inv = 1.0 / 32.0

    xbf = ar_stream.view(0, BF16, DC * SL)
    krot = ar_stream.view(32 * 1024, BF16, DC * SL)
    v_tm = ar_w.view(0, BF16, 16 * 1024)
    w_sb = ar_w.view(32 * 1024, BF16, DC * 1024)
    qrot = ar_w.view(48 * 1024, BF16, DC * 512)
    zb_v = ar_w.view(48 * 1024, BF16, DC * 512)
    pt = ar_w.view(56 * 1024, BF16, 16 * 512)
    zst_v = ar_w.view(56 * 1024, F32, DC * 512)
    qst = ar_w.view(72 * 1024, F32, DC * 512)
    x32t_v = ar_w.view(72 * 1024, F32, DC * 512)
    cosb = ar_w.view(88 * 1024, BF16, 4 * SL)
    sinb = ar_w.view(104 * 1024, BF16, 4 * SL)
    rbc = ar_w.view(120 * 1024, F32, 512)
    outf = [ar_w.view(122 * 1024 + i * 2 * 1024, F32, 512) for i in range(2)]
    bcr_v = ar_w.view(126 * 1024, F32, 512)
    bias_sb = ar_const.view(608, F32, 4)
    ones_mat = ar_const.view(640, BF16, 128)

    v = {
        "zst": zst_v, "zb": zb_v, "bc_m": rbc, "bc_r": bcr_v, "out_f": outf,
        "zsq": ar_stage.view(4 * 1024, BF16, 512),
        "mrow": ar_stage.view(5 * 1024, F32, 512),
        "r_e2": ar_stage.view(7 * 1024, F32, 512),
        "r_msq": ar_stage.view(9 * 1024, F32, 512),
        "r_rstd": ar_stage.view(11 * 1024, F32, 512),
        "ones_bf": ar_const.view(0, BF16, 1),
        "ones_row": ar_const.view(32, F32, 128),
        "eps1": ar_const.view(576, F32, 1),
    }
    v["r_var"] = v["r_e2"]
    sq = v["zsq"]

    t_om = p.emit("vector", lambda e: e.memset(ones_mat.ap(), 1.0), produce=True)
    last["consts"] = last.get("consts", []) + [t_om]

    x_in_r = x32_in.rearrange("(c p) t -> c p t", p=128)
    x_out_r = x32_out.rearrange("(c p) t -> c p t", p=128)
    cos_r = d["rope_cos"].rearrange("(c p) t -> c p t", p=128)
    sin_r = d["rope_sin"].rearrange("(c p) t -> c p t", p=128)
    wq_r = d["l0_wqt"].rearrange("(k p) e -> p k e", p=128)
    wk_r = d["l0_wkt"].rearrange("(k p) e -> p k e", p=128)
    wv_r = d["l0_wvt"].rearrange("(k p) e -> p k e", p=128)

    t_tab = []
    for c in range(4):
        t_tab.append(p.emit("gpsimd", lambda e, c=c:
                            e.dma_start(out=cosb.ap()[:, c * SL:(c + 1) * SL],
                                        in_=cos_r[c]), produce=True, dma=True))
        t_tab.append(p.emit("gpsimd", lambda e, c=c:
                            e.dma_start(out=sinb.ap()[:, c * SL:(c + 1) * SL],
                                        in_=sin_r[c]), produce=True, dma=True))

    def load_w(wr, dep):
        toks = []
        for kc in range(DC):
            toks.append(p.emit("gpsimd", lambda e, kc=kc, wr=wr:
                               e.dma_start(out=w_sb.ap()[:, kc * 1024:(kc + 1) * 1024],
                                           in_=wr[:, kc, :]),
                               deps=dep if kc == 0 else [], produce=True, dma=True))
        return toks

    def rope(dst_view, st_view, tcol, scol, ttoks, pref):
        outs = []
        for c in range(4):
            ta = p.emit("vector", lambda e, c=c:
                        e.tensor_tensor(out=outf[0].ap(),
                                        in0=st_view.ap()[:, c * 512:(c + 1) * 512],
                                        in1=cosb.ap()[:, c * SL + scol: c * SL + scol + 512],
                                        op=OP.mult),
                        deps=ttoks + t_tab + last.get(("outf", 0), [])
                        + last.get(("outf", 1), []), produce=True)
            tb = p.emit("vector", lambda e, c=c:
                        e.tensor_tensor(out=outf[1].ap(),
                                        in0=st_view.ap()[:, (c + 4) * 512:(c + 5) * 512],
                                        in1=sinb.ap()[:, c * SL + scol: c * SL + scol + 512],
                                        op=OP.mult),
                        produce=True)
            t1 = p.emit("vector", lambda e, c=c, tcol=tcol:
                        e.tensor_tensor(out=dst_view.ap()[:, tcol(c)],
                                        in0=outf[0].ap(), in1=outf[1].ap(),
                                        op=OP.add),
                        produce=True)
            tc = p.emit("vector", lambda e, c=c:
                        e.tensor_tensor(out=outf[0].ap(),
                                        in0=st_view.ap()[:, (c + 4) * 512:(c + 5) * 512],
                                        in1=cosb.ap()[:, c * SL + scol: c * SL + scol + 512],
                                        op=OP.mult),
                        produce=True)
            td = p.emit("vector", lambda e, c=c:
                        e.tensor_tensor(out=outf[1].ap(),
                                        in0=st_view.ap()[:, c * 512:(c + 1) * 512],
                                        in1=sinb.ap()[:, c * SL + scol: c * SL + scol + 512],
                                        op=OP.mult),
                        produce=True)
            t2 = p.emit("vector", lambda e, c=c, tcol=tcol:
                        e.tensor_tensor(out=dst_view.ap()[:, tcol(c + 4)],
                                        in0=outf[0].ap(), in1=outf[1].ap(),
                                        op=OP.subtract),
                        produce=True)
            outs.extend([t1, t2])
        last[("outf", 0)] = [outs[-1]]
        last[("outf", 1)] = [outs[-1]]
        return outs

    all_out = []
    n_seq = trunc[0] if trunc else B_LOC
    n_grp = trunc[1] if trunc else NG
    import os
    phase_lim = int(os.environ.get("ATTN0_PHASE_LIMIT", "99"))
    for s in range(n_seq):
        off = s * SL

        seq_join = all_out[-2 * DC:]
        t_x = []
        for c in range(DC):
            t_x.append(p.emit("gpsimd", lambda e, c=c, off=off:
                              e.dma_start(out=xbf.ap()[:, c * SL:(c + 1) * SL],
                                          in_=x_in_r[c, :, off:off + SL]),
                              deps=(last.get("xbf_rd", []) + seq_join if c == 0 else []),
                              produce=True, dma=True))

        if s >= 1 and phase_lim < 2:
            continue
        t_wk = load_w(wk_r, last.get("w_rd", []))
        kn_parts = []
        k_written = []
        for tt in range(4):
            ev = []
            for ec in range(DC):
                bank = ec % 4
                deps0 = ([*t_wk, *t_x] if (tt, ec) == (0, 0) else []) \
                    + [last.get(("bank", bank))]
                tM = None
                for kc in range(DC):
                    tM = p.emit("tensor", lambda e, kc=kc, ec=ec, tt=tt, bank=bank:
                                e.matmul(psum[bank].ap(),
                                         w_sb.ap()[:, kc * 1024 + ec * 128: kc * 1024 + ec * 128 + 128],
                                         xbf.ap()[:, kc * SL + tt * 512: kc * SL + tt * 512 + 512],
                                         start=(kc == 0), stop=(kc == DC - 1)),
                                deps=deps0 if kc == 0 else [],
                                produce=(kc == DC - 1))
                tE = p.emit("vector", lambda e, ec=ec, bank=bank:
                            e.tensor_copy(out=qst.ap()[:, ec * 512:(ec + 1) * 512],
                                          in_=psum[bank].ap()),
                            deps=[tM] + (last.get("qst_rd", []) if ec == 0 else []),
                            produce=True)
                last[("bank", bank)] = tE
                ev.append(tE)
            last["w_rd"] = [tM]
            if dbg is not None and s == 0 and tt == 0:
                tdmp1 = p.emit("sync", lambda e: e.dma_start(out=dbg["xbf"], in_=xbf.ap()[:, 0:2048]),
                               deps=t_x, produce=True, dma=True)
                tdmp2 = p.emit("sync", lambda e: e.dma_start(out=dbg["wsb"], in_=w_sb.ap()),
                               deps=t_wk, produce=True, dma=True)
                tdmp3 = p.emit("sync", lambda e: e.dma_start(out=dbg["qst"], in_=qst.ap()),
                               deps=ev, produce=True, dma=True)
                last["w_rd"] = last["w_rd"] + [tdmp2]
                last["qst_rd"] = last.get("qst_rd", []) + [tdmp3]
                last["xbf_rd"] = last.get("xbf_rd", []) + [tdmp1]
            rk = rope(krot, qst,
                      lambda c, tt=tt: slice(c * SL + tt * 512, c * SL + tt * 512 + 512),
                      tt * 512, ev, "k")
            last["qst_rd"] = rk[-2:]
            k_written.extend(rk)
            tSq = None
            sq_toks = []
            for c in range(DC):
                col = (c % 4) * SL + ((c % 4) if False else 0)
                tQ = p.emit("scalar", lambda e, c=c, tt=tt:
                            e.activation(out=sq.ap(),
                                         in_=krot.ap()[:, c * SL + tt * 512: c * SL + tt * 512 + 512],
                                         func=AF.Square),
                            deps=[rk[c * 2] if c < 4 else rk[(c - 4) * 2 + 1]]
                            + ([last.get("sq_rd")] if c == 0 else []) + ([sq_toks[-1]] if sq_toks else []),
                            produce=True)
                tSq = p.emit("tensor", lambda e, c=c:
                             e.matmul(psum[4].ap(), ones_mat.ap(), sq.ap(),
                                      start=(c == 0), stop=(c == DC - 1)),
                             deps=[tQ] + ([last.get(("bank", 4))] if c == 0 else [])
                             + (last.get("consts", []) if c == 0 else []),
                             produce=True)
                sq_toks.append(tSq)
            last["sq_rd"] = tSq
            tKM = p.emit("vector", lambda e, tt=tt:
                         e.reduce_max(out=bias_sb.ap()[:, 3:4], in_=psum[4].ap(),
                                      axis=mybir.AxisListType.X),
                         deps=[tSq], produce=True, narrow=True)
            last[("bank", 4)] = tKM
            if tt == 0:
                tKM2 = p.emit("vector", lambda e:
                              e.tensor_copy(out=bias_sb.ap()[:, 1:2],
                                            in_=bias_sb.ap()[:, 3:4]),
                              deps=[tKM], produce=True, narrow=True)
            else:
                tKM2 = p.emit("vector", lambda e:
                              e.tensor_tensor(out=bias_sb.ap()[:, 1:2],
                                              in0=bias_sb.ap()[:, 1:2],
                                              in1=bias_sb.ap()[:, 3:4], op=OP.max),
                              deps=[tKM], produce=True, narrow=True)
            kn_parts.append(tKM2)

        if s >= 1 and phase_lim < 3:
            continue
        t_wv = load_w(wv_r, last["w_rd"])
        v_ev = []
        for ti in range(16):
            for eh in range(2):
                bank = (ti * 2 + eh) % 4
                deps0 = (t_wv if (ti, eh) == (0, 0) else []) + [last.get(("bank", bank))] \
                    + (last.get("v_rd", []) if (ti, eh) == (0, 0) else [])
                tM = None
                for kc in range(DC):
                    tM = p.emit("tensor", lambda e, kc=kc, ti=ti, eh=eh, bank=bank:
                                e.matmul(psum[bank].ap(),
                                         xbf.ap()[:, kc * SL + ti * 128: kc * SL + ti * 128 + 128],
                                         w_sb.ap()[:, kc * 1024 + eh * 512: kc * 1024 + eh * 512 + 512],
                                         start=(kc == 0), stop=(kc == DC - 1)),
                                deps=deps0 if kc == 0 else [],
                                produce=(kc == DC - 1))
                tE = p.emit("vector", lambda e, ti=ti, eh=eh, bank=bank:
                            e.tensor_copy(out=v_tm.ap()[:, ti * 1024 + eh * 512: ti * 1024 + eh * 512 + 512],
                                          in_=psum[bank].ap()),
                            deps=[tM], produce=True)
                last[("bank", bank)] = tE
                v_ev.append(tE)
        last["w_rd"] = [tM]

        if s >= 1 and phase_lim < 4:
            continue
        t_wq = load_w(wq_r, last["w_rd"])

        for g in range(n_grp):
            gq = off + g * 512
            nkt = 4 * (g + 1)
            q_ev = []
            for ec in range(DC):
                bank = ec % 4
                deps0 = (t_wq if ec == 0 else []) + [last.get(("bank", bank))] \
                    + (last.get("qst_rd", []) if ec == 0 else []) \
                    + (last.get("qrot_rd", []) if ec == 0 else [])
                tM = None
                for kc in range(DC):
                    tM = p.emit("tensor", lambda e, kc=kc, ec=ec, g=g, bank=bank:
                                e.matmul(psum[bank].ap(),
                                         w_sb.ap()[:, kc * 1024 + ec * 128: kc * 1024 + ec * 128 + 128],
                                         xbf.ap()[:, kc * SL + g * 512: kc * SL + g * 512 + 512],
                                         start=(kc == 0), stop=(kc == DC - 1)),
                                deps=deps0 if kc == 0 else [],
                                produce=(kc == DC - 1))
                tE = p.emit("vector", lambda e, ec=ec, bank=bank:
                            e.tensor_copy(out=qst.ap()[:, ec * 512:(ec + 1) * 512],
                                          in_=psum[bank].ap()),
                            deps=[tM], produce=True)
                last[("bank", bank)] = tE
                q_ev.append(tE)
            if g == NG - 1:
                last["xbf_rd"] = [tM]
                last["w_rd"] = [tM]
            if s >= 1 and phase_lim < 5:
                continue
            rq = rope(qrot, qst, lambda c: slice(c * 512, (c + 1) * 512),
                      g * 512, q_ev, "q")
            last["qst_rd"] = rq[-2:]
            last["qrot_wr"] = rq
            sq_toks = []
            tSq = None
            for c in range(DC):
                tQ = p.emit("scalar", lambda e, c=c:
                            e.activation(out=sq.ap(), in_=qrot.ap()[:, c * 512:(c + 1) * 512],
                                         func=AF.Square),
                            deps=[rq[c * 2] if c < 4 else rq[(c - 4) * 2 + 1]]
                            + ([last.get("sq_rd")] if c == 0 else []) + ([sq_toks[-1][0]] if sq_toks else []),
                            produce=True)
                tSq = p.emit("tensor", lambda e, c=c:
                             e.matmul(psum[4].ap(), ones_mat.ap(), sq.ap(),
                                      start=(c == 0), stop=(c == DC - 1)),
                             deps=[tQ] + ([last.get(("bank", 4))] if c == 0 else [])
                             + (last.get("consts", []) if c == 0 else []),
                             produce=True)
                sq_toks.append((tSq,))
            last["sq_rd"] = tSq
            tQM = p.emit("vector", lambda e:
                         e.reduce_max(out=bias_sb.ap()[:, 2:3], in_=psum[4].ap(),
                                      axis=mybir.AxisListType.X),
                         deps=[tSq], produce=True, narrow=True)
            last[("bank", 4)] = tQM
            tC1 = p.emit("vector", lambda e:
                         e.scalar_tensor_tensor(out=bias_sb.ap()[:, 0:1],
                                                in0=bias_sb.ap()[:, 2:3],
                                                scalar=-1.0 / 64.0,
                                                in1=bias_sb.ap()[:, 1:2],
                                                op0=OP.bypass, op1=OP.add),
                         deps=[tQM, kn_parts[-1]] + last.get("bias_rd", []),
                         produce=True, narrow=True)
            tBCc = p.emit("vector", lambda e:
                          e.tensor_scalar(out=bias_sb.ap()[:, 0:1],
                                          in0=bias_sb.ap()[:, 0:1],
                                          scalar1=-1.0 / 64.0, scalar2=-1.0,
                                          op0=OP.mult, op1=OP.add),
                          deps=[tC1], produce=True, narrow=True)

            if s >= 1 and phase_lim < 6:
                continue
            exp_toks = []
            bias_rd = []
            for kt in range(nkt):
                bank = 6 + kt % 2
                deps0 = [last.get(("bank", bank))]
                if kt == 0:
                    deps0 += last.get("pt_rd", [])
                tM = None
                for c in range(DC):
                    tM = p.emit("tensor", lambda e, c=c, kt=kt, g=g, bank=bank:
                                e.matmul(psum[bank].ap(),
                                         krot.ap()[:, c * SL + kt * 128: c * SL + kt * 128 + 128],
                                         qrot.ap()[:, c * 512:(c + 1) * 512],
                                         start=(c == 0), stop=(c == DC - 1)),
                                deps=(deps0 + k_written[-1:] + rq[-1:]) if c == 0 else [],
                                produce=(c == DC - 1))
                tX = p.emit("scalar", lambda e, kt=kt, bank=bank:
                            e.activation(out=pt.ap()[:, kt * 512:(kt + 1) * 512],
                                         in_=psum[bank].ap(), func=AF.Exp,
                                         scale=sc_inv, bias=bias_sb.ap()[:, 0:1]),
                            deps=[tM, tBCc], produce=True)
                last[("bank", bank)] = tX
                bias_rd.append(tX)
                if kt >= 4 * g:
                    tX = p.emit("gpsimd", lambda e, kt=kt, g=g:
                                e.affine_select(out=pt.ap()[:, kt * 512:(kt + 1) * 512],
                                                in_=pt.ap()[:, kt * 512:(kt + 1) * 512],
                                                compare_op=OP.is_ge, fill=0.0,
                                                base=(512 * g - 128 * kt),
                                                pattern=[[1, 512]],
                                                channel_multiplier=-1),
                                deps=[tX], produce=True)
                exp_toks.append(tX)
            last["bias_rd"] = bias_rd
            last["krot_rd"] = [tM]
            last["qrot_rd"] = [tM]

            tD = None
            for kt in range(nkt):
                tD = p.emit("tensor", lambda e, kt=kt, nkt=nkt:
                            e.matmul(psum[4].ap()[0:1, :], v["ones_bf"].ap(),
                                     pt.ap()[:, kt * 512:(kt + 1) * 512],
                                     start=(kt == 0), stop=(kt == nkt - 1)),
                            deps=[exp_toks[kt]] + ([last.get(("bank", 4))] if kt == 0 else [])
                            + (last.get("consts", []) if kt == 0 else []),
                            produce=(kt == nkt - 1))
            if s >= 1 and phase_lim < 7:
                continue
            tR1 = p.emit("vector", lambda e:
                         e.tensor_scalar(out=v["r_rstd"].ap()[0:1, :],
                                         in0=psum[4].ap()[0:1, :],
                                         scalar1=1e-30, scalar2=None, op0=OP.add),
                         deps=[tD, last.get("ln_rstd_free")], produce=True)
            last[("bank", 4)] = tR1
            tR2 = p.emit("vector", lambda e:
                         e.reciprocal(out=v["r_rstd"].ap()[0:1, :],
                                      in_=v["r_rstd"].ap()[0:1, :]),
                         deps=[], produce=True)
            tRB = p.emit("tensor", lambda e:
                         e.matmul(psum[5].ap(), v["ones_row"].ap()[0:1, :],
                                  v["r_rstd"].ap()[0:1, :], start=True, stop=True),
                         deps=[tR2, last.get(("bank", 5))], produce=True)
            last["ln_rstd_free"] = tRB
            tRBc = p.emit("vector", lambda e:
                          e.tensor_copy(out=rbc.ap(), in_=psum[5].ap()),
                          deps=[tRB] + last.get("rbc_rd", []), produce=True)
            last[("bank", 5)] = tRBc

            bank_order = [0, 1, 2, 3, 6, 7, 4, 5]
            tXT = p.emit("gpsimd", lambda e, gq=gq:
                         e.dma_start(out=x32t_v.ap(),
                                     in_=x_in_r[:, :, gq:gq + 512]
                                     .rearrange("c p t -> p c t")),
                         deps=last.get("qst_rd", []) + last.get("qst_rd2", []),
                         produce=True, dma=True)
            pv_toks = []
            for ci in range(DC):
                bank = bank_order[ci]
                deps0 = [last.get(("bank", bank))]
                tM = None
                for kt in range(nkt):
                    tM = p.emit("tensor", lambda e, kt=kt, oc=ci, bank=bank, nkt=nkt:
                                e.matmul(psum[bank].ap(),
                                         v_tm.ap()[:, kt * 1024 + oc * 128: kt * 1024 + oc * 128 + 128],
                                         pt.ap()[:, kt * 512:(kt + 1) * 512],
                                         start=(kt == 0), stop=(kt == nkt - 1)),
                                deps=(deps0 + [exp_toks[kt], v_ev[-1]]) if kt == 0 else [exp_toks[kt]],
                                produce=(kt == nkt - 1))
                pv_toks.append(tM)
            last["pt_rd"] = [pv_toks[-1]]
            last["v_rd"] = [pv_toks[-1]]
            z_toks = []
            rbc_rd = []
            for ci in range(DC):
                bank = bank_order[ci]
                tZ1 = p.emit("vector", lambda e, oc=ci, bank=bank:
                             e.tensor_tensor(out=zst_v.ap()[:, oc * 512:(oc + 1) * 512],
                                             in0=psum[bank].ap(), in1=rbc.ap(),
                                             op=OP.mult),
                             deps=[pv_toks[-1], tRBc], produce=True)
                rbc_rd.append(tZ1)
                tZ2 = p.emit("vector", lambda e, oc=ci:
                             e.tensor_tensor(out=zst_v.ap()[:, oc * 512:(oc + 1) * 512],
                                             in0=zst_v.ap()[:, oc * 512:(oc + 1) * 512],
                                             in1=x32t_v.ap()[:, oc * 512:(oc + 1) * 512],
                                             op=OP.add),
                             deps=[tXT], produce=True)
                last[("bank", bank)] = tZ2
                z_toks.append(tZ2)
            if dbg is not None and s == 0 and g == 0:
                t1_ = p.emit("sync", lambda e: e.dma_start(out=dbg["krot"], in_=krot.ap()),
                             deps=k_written[-1:], produce=True, dma=True)
                last["krot_rd"] = last.get("krot_rd", []) + [t1_]
                t2_ = p.emit("sync", lambda e: e.dma_start(out=dbg["qrot"], in_=qrot.ap()),
                             deps=rq[-1:], produce=True, dma=True)
                last["qrot_rd"] = last.get("qrot_rd", []) + [t2_]
                t3_ = p.emit("sync", lambda e: e.dma_start(out=dbg["pt"], in_=pt.ap()[:, 0:4 * 512]),
                             deps=exp_toks, produce=True, dma=True)
                t4_ = p.emit("sync", lambda e: e.dma_start(out=dbg["rrow"], in_=v["r_rstd"].ap()[0:1, :]),
                             deps=[tR2, tRB], produce=True, dma=True)
                last["ln_rstd_free"] = t4_
                t5_ = p.emit("sync", lambda e: e.dma_start(out=dbg["vtm"], in_=v_tm.ap()[:, 0:4 * 1024]),
                             deps=[v_ev[-1]], produce=True, dma=True)
                last["v_rd"] = last.get("v_rd", []) + [t5_]
                t6_ = p.emit("sync", lambda e: e.dma_start(out=dbg["bias"], in_=bias_sb.ap()[:, 0:1]),
                             deps=[tBCc] + exp_toks, produce=True, dma=True)
                last["bias_rd"] = last.get("bias_rd", []) + [t6_]
                t7_ = p.emit("sync", lambda e: e.dma_start(out=dbg["zst"], in_=zst_v.ap()),
                             deps=z_toks, produce=True, dma=True)
                last["pt_rd"] = last.get("pt_rd", []) + [t3_, t7_]
                t8_ = p.emit("sync", lambda e: e.dma_start(out=dbg["rt"], in_=bias_sb.ap()[0:1, 0:4]),
                             deps=[tBCc], produce=True, dma=True)

            last["rbc_rd"] = rbc_rd
            last["qst_rd2"] = z_toks[-1:]

            last["ln_s2_read"] = z_toks[-1]
            st = emit_ln_tile(p, v, (off + g * 512) // 512, z_toks, x_out_r, last,
                              psum[4], psum[5], psum[6], psum[7])
            last[("bank", 4)] = last["ln_s2_read"]
            last[("bank", 5)] = last["ln_s1_read"]
            last[("bank", 6)] = last["ln_bc_free"]
            last[("bank", 7)] = last["ln_bc_free"]
            last["pt_rd"] = last.get("pt_rd", []) + [last[("zuse", DC - 1)]]
            last["qrot_rd"] = last.get("qrot_rd", []) + [last["ln_zb_free"]]
            last["qst_rd"] = last.get("qst_rd2", [])
            all_out.extend(st)
    return all_out



def emit_attn1(p, d, ar_stream, ar_w, ar_stage, ar_const, psum, x32_in, x32_out,
               last=None, dbg=None):
    if last is None:
        last = {}
    SL = S
    NG = SL // 512
    SC32 = math.sqrt(32.0)
    scq = 1.0 / SC32

    xbf = ar_stream.view(0, BF16, DC * SL)
    attn_fm = ar_stream.view(0, BF16, DC * SL)
    k_fm = ar_stream.view(32 * 1024, BF16, DC * SL)
    q_fm = ar_w.view(0, BF16, DC * SL)
    v_tm = ar_w.view(32 * 1024, BF16, 16 * 1024)
    w_sb = ar_w.view(64 * 1024, BF16, DC * 1024)
    pt0 = ar_w.view(80 * 1024, BF16, 16 * 512)
    pt1 = ar_w.view(96 * 1024, BF16, 16 * 512)
    r0b = ar_w.view(112 * 1024, F32, 512)
    r1b = ar_w.view(114 * 1024, F32, 512)
    apre = ar_w.view(116 * 1024, F32, 512)
    sqh = ar_w.view(118 * 1024, BF16, 512)
    rmsb = ar_w.view(119 * 1024, F32, 512)
    r0row = ar_w.view(121 * 1024, F32, 512)
    r1row = ar_w.view(123 * 1024, F32, 512)
    biasc = ar_w.view(125 * 1024, F32, 32)
    lrow = ar_w.view(125 * 1024 + 128, F32, 16)
    lprod = ar_w.view(125 * 1024 + 576, BF16, 32)
    rmsw = ar_w.view(125 * 1024 + 512, F32, 2)
    u1tmp = ar_w.view(126 * 1024, F32, 512)
    zst_v = ar_w.view(80 * 1024, F32, DC * 512)
    x32t_v = ar_w.view(96 * 1024, F32, DC * 512)
    zb_v = ar_w.view(32 * 1024, BF16, DC * 512)
    outf = [ar_w.view(40 * 1024 + i * 2 * 1024, F32, 512) for i in range(2)]
    bcm_v = ar_w.view(44 * 1024, F32, 512)
    bcr_v = ar_w.view(46 * 1024, F32, 512)

    v = {
        "zst": zst_v, "zb": zb_v, "bc_m": bcm_v, "bc_r": bcr_v, "out_f": outf,
        "zsq": ar_stage.view(4 * 1024, BF16, 512),
        "mrow": ar_stage.view(5 * 1024, F32, 512),
        "r_e2": ar_stage.view(7 * 1024, F32, 512),
        "r_msq": ar_stage.view(9 * 1024, F32, 512),
        "r_rstd": ar_stage.view(11 * 1024, F32, 512),
        "ones_bf": ar_const.view(0, BF16, 1),
        "ones_row": ar_const.view(32, F32, 128),
        "eps1": ar_const.view(576, F32, 1),
    }
    v["r_var"] = v["r_e2"]
    sq = v["zsq"]
    ones_mat = ar_const.view(640, BF16, 128)
    t_om = p.emit("vector", lambda e: e.memset(ones_mat.ap(), 1.0), produce=True)
    last["consts"] = last.get("consts", []) + [t_om]

    x_in_r = x32_in.rearrange("(c p) t -> c p t", p=128)
    x_out_r = x32_out.rearrange("(c p) t -> c p t", p=128)
    wq_r = d["l1_wq"].rearrange("(k p) e -> p k e", p=128)
    wk_r = d["l1_wk"].rearrange("(k p) e -> p k e", p=128)
    wv_r = d["l1_wv"].rearrange("(k p) e -> p k e", p=128)
    wo_r = d["l1_wo"].rearrange("(k p) e -> p k e", p=128)

    trw = p.emit("gpsimd", lambda e: e.dma_start(out=rmsw.ap()[:, 0:1],
                                                 in_=d["l1_rmsw"]), produce=True, dma=True)
    trw2 = p.emit("vector", lambda e:
                  e.tensor_scalar(out=rmsw.ap()[:, 1:2], in0=rmsw.ap()[:, 0:1],
                                  scalar1=(1.0 - LAMBDA_INIT), scalar2=None,
                                  op0=OP.mult),
                  deps=[trw], produce=True, narrow=True)
    tld = []
    for j, key in enumerate(("l1_lq1", "l1_lk1", "l1_lq2", "l1_lk2")):
        tld.append(p.emit("gpsimd", lambda e, j=j, key=key:
                          e.dma_start(out=lrow.ap()[0:32, j:j + 1],
                                      in_=d[key].rearrange("o t -> (o t)")[:, None]),
                          produce=True, dma=True))
    tp1 = p.emit("vector", lambda e:
                 e.tensor_tensor(out=lprod.ap()[0:32, 0:1], in0=lrow.ap()[0:32, 0:1],
                                 in1=lrow.ap()[0:32, 1:2], op=OP.mult),
                 deps=tld, produce=True, narrow=True)
    ts1 = p.emit("tensor", lambda e:
                 e.matmul(psum[6].ap()[:, 0:1], ones_mat.ap()[0:32, :],
                          lprod.ap()[0:32, 0:1], start=True, stop=True),
                 deps=[tp1, last.get(("bank", 6))] + last.get("consts", []),
                 produce=True)
    tc1 = p.emit("vector", lambda e:
                 e.tensor_copy(out=lrow.ap()[:, 5:6], in_=psum[6].ap()[:, 0:1]),
                 deps=[ts1], produce=True, narrow=True)
    tp2 = p.emit("vector", lambda e:
                 e.tensor_tensor(out=lprod.ap()[0:32, 0:1], in0=lrow.ap()[0:32, 2:3],
                                 in1=lrow.ap()[0:32, 3:4], op=OP.mult),
                 deps=[ts1], produce=True, narrow=True)
    ts2 = p.emit("tensor", lambda e:
                 e.matmul(psum[6].ap()[:, 0:1], ones_mat.ap()[0:32, :],
                          lprod.ap()[0:32, 0:1], start=True, stop=True),
                 deps=[tp2, tc1], produce=True)
    tc2 = p.emit("vector", lambda e:
                 e.tensor_copy(out=lrow.ap()[:, 6:7], in_=psum[6].ap()[:, 0:1]),
                 deps=[ts2], produce=True, narrow=True)
    last[("bank", 6)] = tc2
    te1 = p.emit("scalar", lambda e:
                 e.activation(out=lrow.ap()[:, 7:8], in_=lrow.ap()[:, 5:6],
                              func=AF.Exp),
                 deps=[tc1], produce=True, narrow=True)
    te2 = p.emit("scalar", lambda e:
                 e.activation(out=lrow.ap()[:, 8:9], in_=lrow.ap()[:, 6:7],
                              func=AF.Exp),
                 deps=[tc2], produce=True, narrow=True)
    tl3 = p.emit("vector", lambda e:
                 e.tensor_tensor(out=lrow.ap()[:, 9:10], in0=lrow.ap()[:, 7:8],
                                 in1=lrow.ap()[:, 8:9], op=OP.subtract),
                 deps=[te1, te2], produce=True, narrow=True)
    neg_lam = p.emit("vector", lambda e:
                     e.tensor_scalar(out=lrow.ap()[:, 9:10], in0=lrow.ap()[:, 9:10],
                                     scalar1=LAMBDA_INIT, scalar2=-1.0,
                                     op0=OP.add, op1=OP.mult),
                     deps=[tl3], produce=True, narrow=True)

    all_out = []
    for b in range(B_LOC):
        off = b * SL

        t_x = []
        for c in range(DC):
            t_x.append(p.emit("gpsimd", lambda e, c=c, off=off:
                              e.dma_start(out=xbf.ap()[:, c * SL:(c + 1) * SL],
                                          in_=x_in_r[c, :, off:off + SL]),
                              deps=(last.get("xbf_rd", []) if c == 0 else []),
                              produce=True, dma=True))

        def load_w(wr, dep):
            toks = []
            for kc in range(DC):
                toks.append(p.emit("gpsimd", lambda e, kc=kc, wr=wr:
                                   e.dma_start(out=w_sb.ap()[:, kc * 1024:(kc + 1) * 1024],
                                               in_=wr[:, kc, :]),
                                   deps=dep if kc == 0 else [], produce=True, dma=True))
            return toks

        def proj_fm(dst_view, wtoks, scale, extra_dep, rd_key):
            ev = []
            tM = None
            for tt in range(4):
                for ec in range(DC):
                    bank = (tt * DC + ec) % 4
                    deps0 = (wtoks + t_x + extra_dep if (tt, ec) == (0, 0) else []) \
                        + [last.get(("bank", bank))]
                    for kc in range(DC):
                        tM = p.emit("tensor", lambda e, kc=kc, ec=ec, tt=tt, bank=bank:
                                    e.matmul(psum[bank].ap(),
                                             w_sb.ap()[:, kc * 1024 + ec * 128: kc * 1024 + ec * 128 + 128],
                                             xbf.ap()[:, kc * SL + tt * 512: kc * SL + tt * 512 + 512],
                                             start=(kc == 0), stop=(kc == DC - 1)),
                                    deps=deps0 if kc == 0 else [],
                                    produce=(kc == DC - 1))
                    tE = p.emit("vector", lambda e, ec=ec, tt=tt, bank=bank, tM=tM,
                                dst_view=dst_view, scale=scale:
                                e.tensor_scalar(out=dst_view.ap()[:, ec * SL + tt * 512: ec * SL + tt * 512 + 512],
                                                in0=psum[bank].ap(), scalar1=scale,
                                                scalar2=None, op0=OP.mult),
                                deps=[tM], produce=True)
                    last[("bank", bank)] = tE
                    ev.append(tE)
            last[rd_key] = [tM]
            return ev, tM

        t_wq = load_w(wq_r, last.get("w_rd", []))
        q_ev, tMq = proj_fm(q_fm, t_wq, scq, last.get("q_fm_rd", []), "w_rd")
        t_wk = load_w(wk_r, last["w_rd"])
        k_ev, tMk = proj_fm(k_fm, t_wk, 1.0, last.get("k_fm_rd", []), "w_rd")
        t_wv = load_w(wv_r, last["w_rd"])
        v_ev = []
        tM = None
        for ti in range(16):
            for eh in range(2):
                bank = (ti * 2 + eh) % 4
                deps0 = (t_wv + last.get("v_rd", []) if (ti, eh) == (0, 0) else []) \
                    + [last.get(("bank", bank))]
                for kc in range(DC):
                    tM = p.emit("tensor", lambda e, kc=kc, ti=ti, eh=eh, bank=bank:
                                e.matmul(psum[bank].ap(),
                                         xbf.ap()[:, kc * SL + ti * 128: kc * SL + ti * 128 + 128],
                                         w_sb.ap()[:, kc * 1024 + eh * 512: kc * 1024 + eh * 512 + 512],
                                         start=(kc == 0), stop=(kc == DC - 1)),
                                deps=deps0 if kc == 0 else [],
                                produce=(kc == DC - 1))
                tE = p.emit("vector", lambda e, ti=ti, eh=eh, bank=bank:
                            e.tensor_copy(out=v_tm.ap()[:, ti * 1024 + eh * 512: ti * 1024 + eh * 512 + 512],
                                          in_=psum[bank].ap()),
                            deps=[tM], produce=True)
                last[("bank", bank)] = tE
                v_ev.append(tE)
        last["w_rd"] = [tM]
        last["xbf_rd"] = [tM]

        def chunk_norms(src_view, ev_list, col0):
            toks = []
            for c in range(DC):
                tRmax = None
                for tt in range(4):
                    tQ = p.emit("scalar", lambda e, c=c, tt=tt, src_view=src_view:
                                e.activation(out=sq.ap(),
                                             in_=src_view.ap()[:, c * SL + tt * 512: c * SL + tt * 512 + 512],
                                             func=AF.Square),
                                deps=[ev_list[tt * DC + c], last.get("sq_pe"),
                                      last.get("sq_rd")],
                                produce=True)
                    tSq = p.emit("tensor", lambda e:
                                 e.matmul(psum[4].ap(), ones_mat.ap(), sq.ap(),
                                          start=True, stop=True),
                                 deps=[tQ, last.get(("bank", 4))] + last.get("consts", []),
                                 produce=True)
                    last["sq_pe"] = tSq
                    last["sq_rd"] = tSq
                    tKM = p.emit("vector", lambda e:
                                 e.reduce_max(out=biasc.ap()[:, 24:25],
                                              in_=psum[4].ap(),
                                              axis=mybir.AxisListType.X),
                                 deps=[tSq, last.get("b24_rd")], produce=True, narrow=True)
                    last[("bank", 4)] = tKM
                    if tt == 0:
                        tRmax = p.emit("vector", lambda e, c=c, col0=col0:
                                       e.tensor_copy(out=biasc.ap()[:, col0 + c: col0 + c + 1],
                                                     in_=biasc.ap()[:, 24:25]),
                                       deps=[tKM], produce=True, narrow=True)
                    else:
                        tRmax = p.emit("vector", lambda e, c=c, col0=col0:
                                       e.tensor_tensor(out=biasc.ap()[:, col0 + c: col0 + c + 1],
                                                       in0=biasc.ap()[:, col0 + c: col0 + c + 1],
                                                       in1=biasc.ap()[:, 24:25], op=OP.max),
                                       deps=[tKM], produce=True, narrow=True)
                    last["b24_rd"] = tRmax
                toks.append(tRmax)
            return toks

        qn_toks = chunk_norms(q_fm, q_ev, 0)
        kn_toks = chunk_norms(k_fm, k_ev, 8)
        bias_toks = []
        for c in range(DC):
            t1 = p.emit("vector", lambda e, c=c:
                        e.tensor_scalar(out=biasc.ap()[:, 24:25],
                                        in0=biasc.ap()[:, 8 + c: 9 + c],
                                        scalar1=0.5 / SC32, scalar2=None, op0=OP.mult),
                        deps=[qn_toks[c], kn_toks[c], last.get("b24_rd"),
                              last.get("bias_rd2")] + last.get("pt0_rd", []),
                        produce=True, narrow=True)
            last["b24_rd"] = t1
            t2 = p.emit("vector", lambda e, c=c:
                        e.scalar_tensor_tensor(out=biasc.ap()[:, 16 + c: 17 + c],
                                               in0=biasc.ap()[:, c: c + 1],
                                               scalar=0.5 * SC32,
                                               in1=biasc.ap()[:, 24:25],
                                               op0=OP.mult, op1=OP.add),
                        deps=[t1], produce=True, narrow=True)
            t3 = p.emit("vector", lambda e, c=c:
                        e.tensor_scalar(out=biasc.ap()[:, 16 + c: 17 + c],
                                        in0=biasc.ap()[:, 16 + c: 17 + c],
                                        scalar1=-1.0, scalar2=-1.0,
                                        op0=OP.mult, op1=OP.add),
                        deps=[t2], produce=True, narrow=True)
            last["bias_rd2"] = t3
            bias_toks.append(t3)

        for c in range(DC):
            for pp in range(2):
                h_v = 2 * c + pp
                strips = (64 * pp, 64 * pp + 32)
                for g in range(NG):
                    nkt = 4 * (g + 1)
                    exp0 = []
                    exp1 = []
                    for kt in range(nkt):
                        bk0 = kt % 2
                        bk1 = 2 + kt % 2
                        d0 = [last.get(("bank", bk0))]
                        d1 = [last.get(("bank", bk1))]
                        if kt == 0:
                            d0 = d0 + last.get("pt0_rd", []) + [bias_toks[c], tMq, tMk]
                            d1 = d1 + last.get("pt1_rd", [])
                        tS0 = p.emit("tensor", lambda e, kt=kt, c=c, g=g, s0=strips[0], bk0=bk0:
                                     e.matmul(psum[bk0].ap(),
                                              k_fm.ap()[s0:s0 + 32, c * SL + kt * 128: c * SL + kt * 128 + 128],
                                              q_fm.ap()[s0:s0 + 32, c * SL + g * 512: c * SL + g * 512 + 512],
                                              start=True, stop=True,
                                              tile_position=(s0, 0)),
                                     deps=d0, produce=True)
                        tS1 = p.emit("tensor", lambda e, kt=kt, c=c, g=g, s1=strips[1], bk1=bk1:
                                     e.matmul(psum[bk1].ap(),
                                              k_fm.ap()[s1:s1 + 32, c * SL + kt * 128: c * SL + kt * 128 + 128],
                                              q_fm.ap()[s1:s1 + 32, c * SL + g * 512: c * SL + g * 512 + 512],
                                              start=True, stop=True,
                                              tile_position=(s1, 0)),
                                     deps=d1, produce=True)
                        tX0 = p.emit("scalar", lambda e, kt=kt, c=c, bk0=bk0:
                                     e.activation(out=pt0.ap()[:, kt * 512:(kt + 1) * 512],
                                                  in_=psum[bk0].ap(), func=AF.Exp,
                                                  bias=biasc.ap()[:, 16 + c: 17 + c]),
                                     deps=[tS0], produce=True)
                        tX1 = p.emit("scalar", lambda e, kt=kt, c=c, bk1=bk1:
                                     e.activation(out=pt1.ap()[:, kt * 512:(kt + 1) * 512],
                                                  in_=psum[bk1].ap(), func=AF.Exp,
                                                  bias=biasc.ap()[:, 16 + c: 17 + c]),
                                     deps=[tS1], produce=True)
                        last[("bank", bk0)] = tX0
                        last[("bank", bk1)] = tX1
                        if kt >= 4 * g:
                            tX0 = p.emit("gpsimd", lambda e, kt=kt, g=g:
                                         e.affine_select(out=pt0.ap()[:, kt * 512:(kt + 1) * 512],
                                                         in_=pt0.ap()[:, kt * 512:(kt + 1) * 512],
                                                         compare_op=OP.is_ge, fill=0.0,
                                                         base=(512 * g - 128 * kt),
                                                         pattern=[[1, 512]],
                                                         channel_multiplier=-1),
                                         deps=[tX0], produce=True)
                            tX1 = p.emit("gpsimd", lambda e, kt=kt, g=g:
                                         e.affine_select(out=pt1.ap()[:, kt * 512:(kt + 1) * 512],
                                                         in_=pt1.ap()[:, kt * 512:(kt + 1) * 512],
                                                         compare_op=OP.is_ge, fill=0.0,
                                                         base=(512 * g - 128 * kt),
                                                         pattern=[[1, 512]],
                                                         channel_multiplier=-1),
                                         deps=[tX1], produce=True)
                        exp0.append(tX0)
                        exp1.append(tX1)

                    tD0 = tD1 = None
                    for kt in range(nkt):
                        tD0 = p.emit("tensor", lambda e, kt=kt, nkt=nkt:
                                     e.matmul(psum[4].ap()[0:1, :], v["ones_bf"].ap(),
                                              pt0.ap()[:, kt * 512:(kt + 1) * 512],
                                              start=(kt == 0), stop=(kt == nkt - 1)),
                                     deps=[exp0[kt]] + ([last.get(("bank", 4))] if kt == 0 else [])
                                     + (last.get("consts", []) if kt == 0 else []),
                                     produce=(kt == nkt - 1))
                    for kt in range(nkt):
                        tD1 = p.emit("tensor", lambda e, kt=kt, nkt=nkt:
                                     e.matmul(psum[5].ap()[0:1, :], v["ones_bf"].ap(),
                                              pt1.ap()[:, kt * 512:(kt + 1) * 512],
                                              start=(kt == 0), stop=(kt == nkt - 1)),
                                     deps=[exp1[kt]] + ([last.get(("bank", 5))] if kt == 0 else []),
                                     produce=(kt == nkt - 1))
                    tR0a = p.emit("vector", lambda e:
                                  e.tensor_scalar(out=r0row.ap()[0:1, :],
                                                  in0=psum[4].ap()[0:1, :],
                                                  scalar1=1e-30, scalar2=None, op0=OP.add),
                                  deps=[tD0] + last.get("r0_rd", []), produce=True)
                    last[("bank", 4)] = tR0a
                    tR0 = p.emit("vector", lambda e:
                                 e.reciprocal(out=r0row.ap()[0:1, :], in_=r0row.ap()[0:1, :]),
                                 produce=True)
                    tR1a = p.emit("vector", lambda e:
                                  e.tensor_scalar(out=r1row.ap()[0:1, :],
                                                  in0=psum[5].ap()[0:1, :],
                                                  scalar1=1e-30, scalar2=None, op0=OP.add),
                                  deps=[tD1] + last.get("r1_rd", []), produce=True)
                    last[("bank", 5)] = tR1a
                    tR1 = p.emit("vector", lambda e:
                                 e.reciprocal(out=r1row.ap()[0:1, :], in_=r1row.ap()[0:1, :]),
                                 produce=True)
                    tR1s = p.emit("vector", lambda e:
                                  e.tensor_scalar(out=r1row.ap()[0:1, :], in0=r1row.ap()[0:1, :],
                                                  scalar1=lrow.ap()[0:1, 9:10], scalar2=None,
                                                  op0=OP.mult),
                                  deps=[tR1, neg_lam], produce=True)
                    tB0 = p.emit("tensor", lambda e:
                                 e.matmul(psum[6].ap(), v["ones_row"].ap()[0:1, :],
                                          r0row.ap()[0:1, :], start=True, stop=True),
                                 deps=[tR0, last.get(("bank", 6))], produce=True)
                    tB1 = p.emit("tensor", lambda e:
                                 e.matmul(psum[7].ap(), v["ones_row"].ap()[0:1, :],
                                          r1row.ap()[0:1, :], start=True, stop=True),
                                 deps=[tR1s, last.get(("bank", 7))], produce=True)
                    last["r0_rd"] = [tB0]
                    last["r1_rd"] = [tB1]
                    tB0c = p.emit("vector", lambda e:
                                  e.tensor_copy(out=r0b.ap(), in_=psum[6].ap()),
                                  deps=[tB0] + last.get("r0b_rd", []), produce=True)
                    tB1c = p.emit("vector", lambda e:
                                  e.tensor_copy(out=r1b.ap(), in_=psum[7].ap()),
                                  deps=[tB1] + last.get("r1b_rd", []), produce=True)
                    last[("bank", 6)] = tB0c
                    last[("bank", 7)] = tB1c

                    pvs = []
                    for bank, ptv, expl in ((0, pt0, exp0), (1, pt1, exp1)):
                        tMv = None
                        for kt in range(nkt):
                            tMv = p.emit("tensor", lambda e, kt=kt, nkt=nkt, bank=bank,
                                         ptv=ptv, h_v=h_v, pp=pp:
                                         e.matmul(psum[bank].ap()[64 * pp:64 * pp + 64, :],
                                                  v_tm.ap()[:, kt * 1024 + h_v * 64: kt * 1024 + h_v * 64 + 64],
                                                  ptv.ap()[:, kt * 512:(kt + 1) * 512],
                                                  start=(kt == 0), stop=(kt == nkt - 1),
                                                  tile_position=(0, 64 * pp)),
                                         deps=([last.get(("bank", bank)), v_ev[-1], expl[kt]]
                                               if kt == 0 else [expl[kt]]),
                                         produce=(kt == nkt - 1))
                        pvs.append(tMv)
                    last["pt0_rd"] = [pvs[0]]
                    last["pt1_rd"] = [pvs[1]]

                    rr = 64 * pp
                    tA1 = p.emit("vector", lambda e, rr=rr:
                                 e.tensor_tensor(out=apre.ap()[rr:rr + 64, :],
                                                 in0=psum[0].ap()[rr:rr + 64, :],
                                                 in1=r0b.ap()[rr:rr + 64, :], op=OP.mult),
                                 deps=[pvs[0], pvs[1], tB0c] + last.get("apre_rd", []),
                                 produce=True)
                    tA2a = p.emit("vector", lambda e, rr=rr:
                                  e.tensor_tensor(out=u1tmp.ap()[rr:rr + 64, :],
                                                  in0=psum[1].ap()[rr:rr + 64, :],
                                                  in1=r1b.ap()[rr:rr + 64, :], op=OP.mult),
                                  deps=[tB1c, pvs[1]] + last.get("u1_rd", []),
                                  produce=True)
                    tA2b = p.emit("vector", lambda e, rr=rr:
                                  e.tensor_tensor(out=apre.ap()[rr:rr + 64, :],
                                                  in0=apre.ap()[rr:rr + 64, :],
                                                  in1=u1tmp.ap()[rr:rr + 64, :], op=OP.add),
                                  deps=[tA1, tA2a], produce=True)
                    last["u1_rd"] = [tA2b]
                    last[("bank", 0)] = tA1
                    last[("bank", 1)] = tA2a
                    tSqh = p.emit("scalar", lambda e, rr=rr:
                                  e.activation(out=sqh.ap()[rr:rr + 64, :],
                                               in_=apre.ap()[rr:rr + 64, :], func=AF.Square),
                                  deps=[tA2b, last.get("sqh_rd")], produce=True)
                    tRm = p.emit("tensor", lambda e, rr=rr:
                                 e.matmul(psum[4].ap()[0:1, :],
                                          v["ones_bf"].ap()[rr:rr + 64, 0:1],
                                          sqh.ap()[rr:rr + 64, :],
                                          start=True, stop=True),
                                 deps=[tSqh, last.get(("bank", 4))], produce=True)
                    last["sqh_rd"] = tRm
                    tMean = p.emit("vector", lambda e:
                                   e.tensor_scalar(out=v["r_rstd"].ap()[0:1, :],
                                                   in0=psum[4].ap()[0:1, :],
                                                   scalar1=1.0 / 64.0,
                                                   scalar2=RMS_EPS,
                                                   op0=OP.mult, op1=OP.add),
                                   deps=[tRm, last.get("ln_rstd_free")], produce=True)
                    last[("bank", 4)] = tMean
                    tLn2 = p.emit("scalar", lambda e:
                                  e.activation(out=v["r_rstd"].ap()[0:1, :],
                                               in_=v["r_rstd"].ap()[0:1, :], func=AF.Ln),
                                  deps=[tMean], produce=True)
                    tEx2 = p.emit("scalar", lambda e:
                                  e.activation(out=v["r_rstd"].ap()[0:1, :],
                                               in_=v["r_rstd"].ap()[0:1, :],
                                               func=AF.Exp, scale=-0.5),
                                  deps=[tLn2], produce=True)
                    tRb = p.emit("tensor", lambda e, rr=rr:
                                 e.matmul(psum[5].ap()[rr:rr + 64, :],
                                          v["ones_row"].ap()[0:1, 0:64],
                                          v["r_rstd"].ap()[0:1, :],
                                          start=True, stop=True,
                                          tile_position=(0, rr)),
                                 deps=[tEx2, last.get(("bank", 5))], produce=True)
                    last["ln_rstd_free"] = tRb
                    tRbc = p.emit("vector", lambda e, rr=rr:
                                  e.tensor_copy(out=rmsb.ap()[rr:rr + 64, :],
                                                in_=psum[5].ap()[rr:rr + 64, :]),
                                  deps=[tRb] + last.get("rmsb_rd", []), produce=True)
                    last[("bank", 5)] = tRbc
                    tFin = p.emit("vector", lambda e, rr=rr, c=c, g=g:
                                  e.scalar_tensor_tensor(
                                      out=attn_fm.ap()[rr:rr + 64, c * SL + g * 512: c * SL + g * 512 + 512],
                                      in0=apre.ap()[rr:rr + 64, :],
                                      scalar=rmsw.ap()[rr:rr + 64, 1:2],
                                      in1=rmsb.ap()[rr:rr + 64, :],
                                      op0=OP.mult, op1=OP.mult),
                                  deps=[tRbc, trw2], produce=True)
                    last["rmsb_rd"] = [tFin]
                    last["apre_rd"] = [tSqh, tFin]
                    last[("attn_done", c, pp, g)] = tFin
                    if dbg is not None and b == 0 and (c, pp, g) == (7, 1, 3):
                        tdp = p.emit("sync", lambda e: e.dma_start(out=dbg["pt0"], in_=pt0.ap()),
                                     deps=[pvs[0], pvs[1]], produce=True, dma=True)
                        last["pt0_rd"] = last.get("pt0_rd", []) + [tdp]
                        tdp1 = p.emit("sync", lambda e: e.dma_start(out=dbg["pt1"], in_=pt1.ap()),
                                      deps=[pvs[0], pvs[1]], produce=True, dma=True)
                        last["pt1_rd"] = last.get("pt1_rd", []) + [tdp1]
                        tda = p.emit("sync", lambda e: e.dma_start(out=dbg["apre"], in_=apre.ap()),
                                     deps=[tSqh, tFin], produce=True, dma=True)
                        last["apre_rd"] = last.get("apre_rd", []) + [tda]
                        tdr = p.emit("sync", lambda e: e.dma_start(out=dbg["r0b"], in_=r0b.ap()),
                                     deps=[tA1], produce=True, dma=True)
                        last["r0b_rd"] = last.get("r0b_rd", []) + [tdr]

        t_wo = load_w(wo_r, last["w_rd"])
        attn_done = [last[("attn_done", c2_, pp2, g2)] for c2_ in range(DC)
                     for pp2 in range(2) for g2 in range(NG)]
        if dbg is not None and b == 0:
            td1 = p.emit("sync", lambda e: e.dma_start(out=dbg["attnfm"], in_=attn_fm.ap()),
                         deps=attn_done, produce=True, dma=True)
            last["xbf_rd"] = last.get("xbf_rd", []) + [td1]
            td2 = p.emit("sync", lambda e: e.dma_start(out=dbg["lrow"], in_=lrow.ap()[0:1, 0:16]),
                         deps=[neg_lam], produce=True, dma=True)
            td3 = p.emit("sync", lambda e: e.dma_start(out=dbg["biasc"], in_=biasc.ap()[:, 0:32]),
                         deps=attn_done, produce=True, dma=True)
            last["bias_rd2"] = td3
        for tt in range(4):
            tx = p.emit("gpsimd", lambda e, off=off, tt=tt:
                        e.dma_start(out=x32t_v.ap(),
                                    in_=x_in_r[:, :, off + tt * 512: off + tt * 512 + 512]
                                    .rearrange("c p t -> p c t")),
                        deps=last.get("x32t_rd", []) + (last.get("pt1_rd", []) if tt == 0 else []),
                        produce=True, dma=True)
            z_toks = []
            tZl = None
            for oc in range(DC):
                bank = oc % 4
                mdeps = (t_wo + attn_done + [tx] if oc == 0 else []) + [last.get(("bank", bank))]
                tMo = None
                for kc in range(DC):
                    tMo = p.emit("tensor", lambda e, kc=kc, oc=oc, tt=tt, bank=bank:
                                 e.matmul(psum[bank].ap(),
                                          w_sb.ap()[:, kc * 1024 + oc * 128: kc * 1024 + oc * 128 + 128],
                                          attn_fm.ap()[:, kc * SL + tt * 512: kc * SL + tt * 512 + 512],
                                          start=(kc == 0), stop=(kc == DC - 1)),
                                 deps=mdeps if kc == 0 else [],
                                 produce=(kc == DC - 1))
                tZ = p.emit("vector", lambda e, oc=oc, bank=bank:
                            e.tensor_tensor(out=zst_v.ap()[:, oc * 512:(oc + 1) * 512],
                                            in0=psum[bank].ap(),
                                            in1=x32t_v.ap()[:, oc * 512:(oc + 1) * 512],
                                            op=OP.add),
                            deps=[tMo, tx, last.get(("zuse", oc))]
                            + (last.get("pt0_rd", []) if (tt, oc) == (0, 0) else []),
                            produce=True)
                last[("bank", bank)] = tZ
                z_toks.append(tZ)
                tZl = tZ
            last["x32t_rd"] = [tZl]
            last["ln_s2_read"] = tZl
            st = emit_ln_tile(p, v, (off + tt * 512) // 512, z_toks, x_out_r, last,
                              psum[4], psum[5], psum[6], psum[7])
            last[("bank", 4)] = last["ln_s2_read"]
            last[("bank", 5)] = last["ln_s1_read"]
            last[("bank", 6)] = last["ln_bc_free"]
            last[("bank", 7)] = last["ln_bc_free"]
            last["pt0_rd"] = last.get("pt0_rd", []) + [last[("zuse", DC - 1)]]
            all_out.extend(st)
        last["q_fm_rd"] = list(last.get("pt0_rd", []))
        last["k_fm_rd"] = list(last.get("pt0_rd", []))
        last["v_rd"] = list(last.get("pt0_rd", []))
        last["xbf_rd"] = list(last.get("pt0_rd", []))
        last["w_rd"] = list(last.get("pt0_rd", []))
    return all_out


def build_ffn_launch(L):
    nc = bass.Bass("TRN2", target_bir_lowering=False)
    pre = f"l{L}_"
    d = {}
    d_x_in = nc.dram_tensor("x_in", [D, T], F32, kind="ExternalInput").ap()
    d_x_out = nc.dram_tensor("x_out", [D, T], F32, kind="ExternalOutput").ap()
    for key, shape in (("w1t", [D, DH]), ("w2t", [D, DH]), ("h1t", [DH, DH]),
                       ("h2t", [DH, DH]), ("w3t", [DH, D])):
        d[pre + key] = nc.dram_tensor(pre + key, shape, F32, kind="ExternalInput").ap()
    for key, n in (("s1", HC), ("s2", HC), ("sh1", HC), ("sh2", HC), ("s3", DC)):
        d[pre + key] = nc.dram_tensor(pre + key, [128, n], F32, kind="ExternalInput").ap()
    for key in ("h0", "h1", "h2"):
        d[pre + key] = nc.dram_tensor(pre + key, [HC, 128, T], BF16, kind="Internal").ap()

    with contextlib.ExitStack() as stack:
        ar_stream = Arena(nc, stack, "ar_stream", 64 * 1024)
        ar_w = Arena(nc, stack, "ar_w", 128 * 1024)
        ar_stage = Arena(nc, stack, "ar_stage", 13 * 1024)
        ar_const = Arena(nc, stack, "ar_const", 2 * 1024)
        psum = [stack.enter_context(nc.psum_tensor(f"ps{i}", [128, 512], F32))
                for i in range(8)]
        p = Prog(nc, stack)
        out_toks = emit_ffn(p, d, L, ar_stream, ar_w, ar_stage, ar_const, psum,
                            d_x_in, d_x_out)
        p.emit("sync", lambda e: e.nop(), deps=out_toks)
        blk = stack.enter_context(nc.Block())
        p.replay(blk)
    return nc


def ffn_in_map(pr, L, x_fm_core):
    m = {"x_in": np.ascontiguousarray(x_fm_core)}
    for key in ("w1t", "w2t", "h1t", "h2t", "w3t", "s1", "s2", "sh1", "sh2", "s3"):
        m[f"l{L}_{key}"] = pr[f"l{L}_{key}"]
    return m


def build_attn0_launch(debug=False, trunc=False):
    nc = bass.Bass("TRN2", target_bir_lowering=False)
    d = {}
    dbg = None
    d_x_in = nc.dram_tensor("x_in", [D, T], F32, kind="ExternalInput").ap()
    d_x_out = nc.dram_tensor("x_out", [D, T], F32, kind="ExternalOutput").ap()
    if debug:
        dbg = {
            "krot": nc.dram_tensor("dbg_krot", [128, DC * S], BF16, kind="ExternalOutput").ap(),
            "qrot": nc.dram_tensor("dbg_qrot", [128, DC * 512], BF16, kind="ExternalOutput").ap(),
            "pt": nc.dram_tensor("dbg_pt", [128, 4 * 512], BF16, kind="ExternalOutput").ap(),
            "rrow": nc.dram_tensor("dbg_rrow", [1, 512], F32, kind="ExternalOutput").ap(),
            "vtm": nc.dram_tensor("dbg_vtm", [128, 4 * 1024], BF16, kind="ExternalOutput").ap(),
            "bias": nc.dram_tensor("dbg_bias", [128, 1], F32, kind="ExternalOutput").ap(),
            "zst": nc.dram_tensor("dbg_zst", [128, DC * 512], F32, kind="ExternalOutput").ap(),
            "xbf": nc.dram_tensor("dbg_xbf", [128, 2048], BF16, kind="ExternalOutput").ap(),
            "wsb": nc.dram_tensor("dbg_wsb", [128, DC * 1024], BF16, kind="ExternalOutput").ap(),
            "qst": nc.dram_tensor("dbg_qst", [128, DC * 512], F32, kind="ExternalOutput").ap(),
            "rt": nc.dram_tensor("dbg_rt", [1, 4], F32, kind="ExternalOutput").ap(),
        }
    for key, shape in (("l0_wqt", [D, D]), ("l0_wkt", [D, D]), ("l0_wvt", [D, D]),
                       ("rope_cos", [D // 2, S]), ("rope_sin", [D // 2, S])):
        d[key] = nc.dram_tensor(key, shape, F32, kind="ExternalInput").ap()

    with contextlib.ExitStack() as stack:
        ar_stream = Arena(nc, stack, "ar_stream", 64 * 1024)
        ar_w = Arena(nc, stack, "ar_w", 128 * 1024)
        ar_stage = Arena(nc, stack, "ar_stage", 13 * 1024)
        ar_const = Arena(nc, stack, "ar_const", 2 * 1024)
        psum = [stack.enter_context(nc.psum_tensor(f"ps{i}", [128, 512], F32))
                for i in range(8)]
        p = Prog(nc, stack)
        last = {}
        v_ones_bf = ar_const.view(0, BF16, 1)
        v_ones_row = ar_const.view(32, F32, 128)
        v_eps1 = ar_const.view(576, F32, 1)
        t_c1 = p.emit("vector", lambda e: e.memset(v_ones_bf.ap(), 1.0), produce=True)
        t_c2 = p.emit("vector", lambda e: e.memset(v_ones_row.ap()[0:1, :], 1.0), produce=True)
        t_c3 = p.emit("vector", lambda e: e.memset(v_eps1.ap()[0:1, :], LN_EPS), produce=True)
        last["consts"] = [t_c1, t_c2, t_c3]
        out_toks = emit_attn0(p, d, ar_stream, ar_w, ar_stage, ar_const, psum,
                              d_x_in, d_x_out, last=last, dbg=dbg, trunc=trunc)
        p.emit("sync", lambda e: e.nop(), deps=out_toks)
        blk = stack.enter_context(nc.Block())
        p.replay(blk)
    return nc


def attn0_in_map(pr, x_fm_core):
    return {"x_in": np.ascontiguousarray(x_fm_core),
            "l0_wqt": pr["l0_wqt"], "l0_wkt": pr["l0_wkt"], "l0_wvt": pr["l0_wvt"],
            "rope_cos": pr["rope_cos"], "rope_sin": pr["rope_sin"]}


def build_attn1_launch(debug=False):
    nc = bass.Bass("TRN2", target_bir_lowering=False)
    d = {}
    dbg = None
    d_x_in = nc.dram_tensor("x_in", [D, T], F32, kind="ExternalInput").ap()
    d_x_out = nc.dram_tensor("x_out", [D, T], F32, kind="ExternalOutput").ap()
    if debug:
        dbg = {"attnfm": nc.dram_tensor("dbg_attnfm", [128, DC * S], BF16, kind="ExternalOutput").ap(),
               "lrow": nc.dram_tensor("dbg_lrow", [1, 16], F32, kind="ExternalOutput").ap(),
               "biasc": nc.dram_tensor("dbg_biasc", [128, 32], F32, kind="ExternalOutput").ap(),
               "pt0": nc.dram_tensor("dbg_pt0", [128, 16 * 512], BF16, kind="ExternalOutput").ap(),
               "pt1": nc.dram_tensor("dbg_pt1", [128, 16 * 512], BF16, kind="ExternalOutput").ap(),
               "apre": nc.dram_tensor("dbg_apre", [128, 512], F32, kind="ExternalOutput").ap(),
               "r0b": nc.dram_tensor("dbg_r0b", [128, 512], F32, kind="ExternalOutput").ap()}
    for key, shape in (("l1_wq", [D, D]), ("l1_wk", [D, D]), ("l1_wv", [D, D]),
                       ("l1_wo", [D, D]), ("l1_lq1", [1, 32]), ("l1_lk1", [1, 32]),
                       ("l1_lq2", [1, 32]), ("l1_lk2", [1, 32]), ("l1_rmsw", [128, 1])):
        d[key] = nc.dram_tensor(key, shape, F32, kind="ExternalInput").ap()

    with contextlib.ExitStack() as stack:
        ar_stream = Arena(nc, stack, "ar_stream", 64 * 1024)
        ar_w = Arena(nc, stack, "ar_w", 128 * 1024)
        ar_stage = Arena(nc, stack, "ar_stage", 13 * 1024)
        ar_const = Arena(nc, stack, "ar_const", 2 * 1024)
        psum = [stack.enter_context(nc.psum_tensor(f"ps{i}", [128, 512], F32))
                for i in range(8)]
        p = Prog(nc, stack)
        last = {}
        v_ones_bf = ar_const.view(0, BF16, 1)
        v_ones_row = ar_const.view(32, F32, 128)
        v_eps1 = ar_const.view(576, F32, 1)
        t_c1 = p.emit("vector", lambda e: e.memset(v_ones_bf.ap(), 1.0), produce=True)
        t_c2 = p.emit("vector", lambda e: e.memset(v_ones_row.ap()[0:1, :], 1.0), produce=True)
        t_c3 = p.emit("vector", lambda e: e.memset(v_eps1.ap()[0:1, :], LN_EPS), produce=True)
        last["consts"] = [t_c1, t_c2, t_c3]
        out_toks = emit_attn1(p, d, ar_stream, ar_w, ar_stage, ar_const, psum,
                              d_x_in, d_x_out, last=last, dbg=dbg)
        p.emit("sync", lambda e: e.nop(), deps=out_toks)
        blk = stack.enter_context(nc.Block())
        p.replay(blk)
    return nc


def attn1_in_map(pr, x_fm_core):
    return {"x_in": np.ascontiguousarray(x_fm_core),
            "l1_wq": pr["l1_wq"], "l1_wk": pr["l1_wk"], "l1_wv": pr["l1_wv"],
            "l1_wo": pr["l1_wo"], "l1_lq1": pr["l1_lq1"], "l1_lk1": pr["l1_lk1"],
            "l1_lq2": pr["l1_lq2"], "l1_lk2": pr["l1_lk2"], "l1_rmsw": pr["l1_rmsw"]}


_NC_CACHE = {}


def _get_nc(name, builder):
    if name not in _NC_CACHE:
        _NC_CACHE[name] = builder()
    return _NC_CACHE[name]


def kernel(**inputs):
    pr = host_prep(inputs)
    cores = list(range(NCORES))

    nc = _get_nc("attn0", build_attn0_launch)
    in_maps = [attn0_in_map(pr, pr["x_fm"][c]) for c in cores]
    res = run_bass_kernel_spmd(nc, in_maps, core_ids=cores)
    x1 = [np.asarray(res.results[c]["x_out"]) for c in cores]

    nc = _get_nc("ffn0", lambda: build_ffn_launch(0))
    in_maps = [ffn_in_map(pr, 0, x1[c]) for c in cores]
    res = run_bass_kernel_spmd(nc, in_maps, core_ids=cores)
    x2 = [np.asarray(res.results[c]["x_out"]) for c in cores]

    nc = _get_nc("attn1", build_attn1_launch)
    in_maps = [attn1_in_map(pr, x2[c]) for c in cores]
    res = run_bass_kernel_spmd(nc, in_maps, core_ids=cores)
    x3 = [np.asarray(res.results[c]["x_out"]) for c in cores]

    nc = _get_nc("ffn1", lambda: build_ffn_launch(1))
    in_maps = [ffn_in_map(pr, 1, x3[c]) for c in cores]
    res = run_bass_kernel_spmd(nc, in_maps, core_ids=cores)
    out = np.empty((B, S, D), dtype=np.float32)
    for c in cores:
        xo = np.asarray(res.results[c]["x_out"])
        out[c * B_LOC:(c + 1) * B_LOC] = xo.T.reshape(B_LOC, S, D)
    return out
